# version 07 (final)
# baseline (speedup 1.0000x reference)
import numpy as np
import concourse.bass as bass
import concourse.mybir as mybir
from concourse.ap import AP
from concourse.bass_utils import run_bass_kernel_spmd

F32 = mybir.dt.float32
BF16 = mybir.dt.bfloat16
AF = mybir.ActivationFunctionType
ALU = mybir.AluOpType
ET = mybir.EngineType
AX = mybir.AxisListType

NCORES = 8
D = 1024
SQ = 8192
BATCH = 2
TTOT = BATCH * SQ
TCORE = 2048
NG = 8
L = 128
KCH = SQ // L
NST = 16
NCH = 128
NSG = 16
TAU = 512
NTAU = 4
NEXP = 8
FF = 2048
NTT = 16

LN_EPS = 1e-5
RMS_EPS = 1e-6
TOPK_EPS = 1e-9

SSM_FULL = True


class EagerTok:
    __slots__ = ("sem", "value", "eng", "used")

    def __init__(self, sem, value, eng):
        self.sem = sem
        self.value = value
        self.eng = eng
        self.used = True


class TokRef:
    __slots__ = ("eng", "sem", "inc", "used", "value")

    def __init__(self, eng, sem, inc, eager):
        self.eng = eng
        self.sem = sem
        self.inc = inc
        self.used = eager
        self.value = None


class Bld:
    def __init__(self, debug=False, stage=5, detect_races=True):
        self.stage = stage
        self.debug = debug
        self.nc = bass.Bass("TRN2", target_bir_lowering=False, num_devices=NCORES,
                            detect_race_conditions=detect_races)
        self.ops = {k: [] for k in ("pe", "dve", "act", "pool", "sp")}
        self.cnt = {}
        self.semh = {}
        self.pruned = {k: {} for k in self.ops}
        self.bank_rd = {}
        self.bank_wr = {}

    def _sem(self, name):
        if name not in self.semh:
            self.semh[name] = self.nc.alloc_semaphore(name)
            self.cnt[name] = 0
        return self.semh[name]

    def emit(self, eng, fn, waits=(), sem=None, inc=0):
        if sem:
            self._sem(sem)
        w = []
        for tok in waits:
            if tok is None:
                continue
            if isinstance(tok, EagerTok):
                w.append(tok)
                continue
            if tok.eng == eng == "pe":
                continue
            tok.used = True
            w.append(tok)
        tok_out = None
        if sem:
            eager = inc == 16
            tok_out = TokRef(eng, sem, inc, eager)
        self.ops[eng].append((tuple(w), fn, tok_out))
        return tok_out

    DMA_SEMS = 8

    def dma(self, q, out, in_, waits=(), **kw):
        if not hasattr(self, "_dmacnt"):
            self._dmacnt = {}
            self._dmause = {}
        i = self._dmacnt.get(q, 0)
        self._dmacnt[q] = i + 1
        sem = f"{q}d{i % self.DMA_SEMS}"
        self._sem(sem)
        nprev = self._dmause.get(sem, 0)
        self._dmause[sem] = nprev + 1
        w = list(waits)
        if nprev:
            w.append(EagerTok(sem, 16 * nprev, "dma"))
        return self.emit(q, lambda e: e.dma_start(out=out, in_=in_, **kw),
                         w, sem=sem, inc=16)

    def pe(self, fn, waits=(), token=True):
        return self.emit("pe", fn, waits, sem="pe" if token else None, inc=1)

    def dve(self, fn, waits=(), token=True):
        return self.emit("dve", fn, waits, sem="dve" if token else None, inc=1)

    def act(self, fn, waits=(), token=True):
        return self.emit("act", fn, waits, sem="actc" if token else None, inc=1)

    def pool(self, fn, waits=(), token=True):
        return self.emit("pool", fn, waits, sem="poolc" if token else None, inc=1)

    def bw(self, i):
        t = self.bank_rd.get(i)
        return [t] if t else []

    def brd(self, i):
        t = self.bank_wr.get(i)
        return [t] if t else []

    def wrote(self, i, tok):
        self.bank_wr[i] = tok

    def read(self, i, tok):
        self.bank_rd[i] = tok

    def finish(self):
        nc = self.nc
        for key in self.ops:
            for waits, fn, tok in self.ops[key]:
                pass
        for key in self.ops:
            for waits, fn, tok in self.ops[key]:
                if tok is not None and tok.used:
                    self.cnt[tok.sem] += tok.inc
                    tok.value = self.cnt[tok.sem]
        eng_map = {"pe": "tensor", "dve": "vector", "act": "scalar",
                   "pool": "gpsimd", "sp": "sync"}
        with nc.Block() as block:
            for key, attr in eng_map.items():
                ops = self.ops[key]
                if not ops:
                    continue
                def make_body(ops=ops, key=key):
                    def body(eng):
                        pruned = {}
                        for waits, fn, tok in ops:
                            for wv in waits:
                                s, v = wv.sem, wv.value
                                assert v is not None
                                if pruned.get(s, 0) >= v:
                                    continue
                                pruned[s] = v
                                eng.wait_ge(self.semh[s], v)
                            inst = fn(eng)
                            if tok is not None and tok.used:
                                assert inst is not None, f"op with sem {tok.sem} returned None"
                                inst.then_inc(self.semh[tok.sem], tok.inc)
                    return body
                getattr(block, attr)(make_body())

    def sb(self, name, shape, dtype):
        return self.nc.alloc_sbuf_tensor(name, shape, dtype)

    def build(self):
        nc = self.nc
        self.xs = nc.dram_tensor("xs", [TCORE, D], F32, kind="ExternalInput").ap()
        self.A_s = nc.dram_tensor("A_s", [NCH, NST, NST], F32, kind="ExternalInput").ap()
        self.Bm_s = nc.dram_tensor("Bm_s", [NCH, NST], F32, kind="ExternalInput").ap()
        self.Cm_s = nc.dram_tensor("Cm_s", [NCH, NST], F32, kind="ExternalInput").ap()
        self.Dm_s = nc.dram_tensor("Dm_s", [NCH], F32, kind="ExternalInput").ap()
        self.ln_w = nc.dram_tensor("ln_w", [D], F32, kind="ExternalInput").ap()
        self.ln_b = nc.dram_tensor("ln_b", [D], F32, kind="ExternalInput").ap()
        self.W_out = nc.dram_tensor("W_out", [D, D], F32, kind="ExternalInput").ap()
        self.b_out = nc.dram_tensor("b_out", [D], F32, kind="ExternalInput").ap()
        self.norm_w = nc.dram_tensor("norm_w", [D], F32, kind="ExternalInput").ap()
        self.router_w = nc.dram_tensor("router_w", [D, NEXP], F32, kind="ExternalInput").ap()
        self.w1 = nc.dram_tensor("w1", [NEXP, D, FF], F32, kind="ExternalInput").ap()
        self.w3 = nc.dram_tensor("w3", [NEXP, D, FF], F32, kind="ExternalInput").ap()
        self.w2 = nc.dram_tensor("w2", [NEXP, FF, D], F32, kind="ExternalInput").ap()
        self.c_idf = nc.dram_tensor("c_idf", [128, 128], F32, kind="ExternalInput").ap()
        self.c_idb = nc.dram_tensor("c_idb", [128, 128], BF16, kind="ExternalInput").ap()
        self.c_mask8 = nc.dram_tensor("c_mask8", [128, 8], F32, kind="ExternalInput").ap()
        self.c_mask128 = nc.dram_tensor("c_mask128", [128, 128], F32, kind="ExternalInput").ap()
        self.c_ones = nc.dram_tensor("c_ones", [2, 128], F32, kind="ExternalInput").ap()
        self.o_slice = nc.dram_tensor("o_slice", [TCORE, D], F32, kind="ExternalOutput").ap()
        if self.debug:
            self.dbg_z1 = nc.dram_tensor("dbg_z1", [NG, 128, TCORE], F32, kind="ExternalOutput").ap()
            self.dbg_y = nc.dram_tensor("dbg_y", [NG, 128, TCORE], F32, kind="ExternalOutput").ap()
            self.dbg_xn = nc.dram_tensor("dbg_xn", [NCORES, NCH, TCORE], BF16, kind="ExternalOutput").ap()
            self.dbg_pk = nc.dram_tensor("dbg_pk", [NCH, 256], BF16, kind="ExternalOutput").ap()
            self.dbg_yin = nc.dram_tensor("dbg_yin", [NCH, TTOT], F32, kind="ExternalOutput").ap()
            self.dbg_xnin = nc.dram_tensor("dbg_xnin", [D, TCORE], BF16, kind="ExternalOutput").ap()
            self.dbg_rows = nc.dram_tensor("dbg_rows", [2, TCORE], F32, kind="ExternalOutput").ap()
            self.dbg_st = nc.dram_tensor("dbg_st", [NG, 128, TCORE], F32, kind="ExternalOutput").ap()
            self.dbg_sc = nc.dram_tensor("dbg_sc", [128, 96], F32, kind="ExternalOutput").ap()
            self.dbg_sr = nc.dram_tensor("dbg_sr", [32, 128], F32, kind="ExternalOutput").ap()
            self.dbg_cst = nc.dram_tensor("dbg_cst", [128, 2], F32, kind="ExternalOutput").ap()
            self.dbg_sc2 = nc.dram_tensor("dbg_sc2", [128, 96], F32, kind="ExternalOutput").ap()
            self.dbg_steps = nc.dram_tensor("dbg_steps", [6, 128, 16], F32, kind="ExternalOutput").ap()

        self.cc_xn_in = nc.dram_tensor("cc_xn_in", [D, TCORE], BF16).ap()
        self.cc_xn_out = nc.dram_tensor("cc_xn_out", [NCORES * D, TCORE], BF16).ap()
        self.cc_y_in = nc.dram_tensor("cc_y_in", [NCH, TTOT], F32).ap()
        self.cc_y_out = nc.dram_tensor("cc_y_out", [NCORES * NCH, TTOT], F32).ap()
        self.pad_k = nc.dram_tensor("pad_k", [NCH, 256], BF16).ap()
        self.rows_scr = nc.dram_tensor("rows_scr", [32, 128], F32).ap()
        self.xn_my = nc.dram_tensor("xn_my", [NCORES, NCH, TCORE], BF16).ap()
        self.y_my = nc.dram_tensor("y_my", [D, TCORE], F32).ap()
        self.x1_dram = nc.dram_tensor("x1_dram", [NG, 128, TCORE], F32).ap()

        self.stream = [self.sb(f"stream{g}", [128, TCORE], F32) for g in range(NG)]
        self.bigA = self.sb("bigA", [128, 16384], BF16)
        self.warena = self.sb("warena", [128, 20480], BF16)
        self.w2buf = self.sb("w2buf", [128, 8192], BF16)
        self.rowA = self.sb("rowA", [1, TCORE], F32)
        self.rowB = self.sb("rowB", [1, TCORE], F32)
        self.rowC = self.sb("rowC", [1, TCORE], F32)
        self.rowD = self.sb("rowD", [1, TCORE], F32)
        self.cw8 = self.sb("cw8", [8, TCORE], BF16)
        self.cwrow = self.sb("cwrow", [1, TCORE], BF16)
        self.cwbc = self.sb("cwbc", [128, 1024], BF16)
        self.cwf = self.sb("cwf", [128, 128], F32)
        self.silu_t = self.sb("silu_t", [128, 512], BF16)
        self.glu_t = self.sb("glu_t", [128, 512], BF16)
        self.sq_bf = self.sb("sq_bf", [128, 512], BF16)
        self.sq_bf2 = self.sb("sq_bf2", [128, 512], BF16)
        self.ap32 = [self.sb(f"ap32_{i}", [128, 512], F32).ap() for i in range(2)]
        self.ident_f = self.sb("ident_f", [128, 128], F32)
        self.ident_b = self.sb("ident_b", [128, 128], BF16)
        self.ones1_f = self.sb("ones1_f", [1, 128], F32)
        self.ones1_b = self.sb("ones1_b", [1, 128], BF16)
        self.onescol_b = self.sb("onescol_b", [128, 1], BF16)
        self.mask8 = self.sb("mask8", [128, 8], F32)
        self.mask128 = self.sb("mask128", [128, 128], F32)
        self.sparams = self.sb("sparams", [128, 32], F32)
        self.pstack = self.sb("pstack", [32, 128], F32)
        self.statc = self.sb("statc", [128, 96], F32)
        self.srow16 = self.sb("srow16", [32, 128], F32)
        self.cst = self.sb("cst", [128, 2], F32)
        self.rw_bf = self.sb("rw_bf", [128, NEXP * NG], BF16)

        waf = self.warena.ap().bitcast(F32)
        self.waf = waf
        self.tmb = [waf[:, 1024 * i:1024 * (i + 1)] for i in range(2)]
        self.att = [waf[:, 2048 + 512 * g:2048 + 512 * (g + 1)] for g in range(NG)]
        self.ytile = [waf[:, 6144 + 512 * g:6144 + 512 * (g + 1)] for g in range(NG)]
        wab = self.warena.ap()
        self.xg = [wab[:, 1024 * i:1024 * (i + 1)] for i in range(2)]
        self.hank = [wab[:, 2048 + 128 * i:2048 + 128 * (i + 1)] for i in range(2)]
        self.vrev = wab[:, 2304:2432]
        self.wflat = wab[:, 2432:2560]
        self.xmask = wab[:, 2560:2688]
        self.zrow_b = [wab[:, 2688 + 128 * i:2688 + 128 * (i + 1)] for i in range(2)]
        self.wo_stage = wab[:, 7424:8448]
        self.mt = [waf[:, 2048 + 128 * i:2048 + 128 * (i + 1)] for i in range(4)]
        self.scanb = [waf[:, 2560 + 192 * i:2560 + 192 * (i + 1)] for i in range(2)]
        self.vfwd = waf[:, 2944:3072]
        self.wbd = waf[:, 3072:3200]
        self.sgev = waf[:, 3200:3328]
        self.ysb = [waf[:, 3328 + 128 * i:3328 + 128 * (i + 1)] for i in range(2)]
        self.acomp = waf[:, 3584:3600]
        self.bcol = waf[:, 3600:3601]
        self.ccol = waf[:, 3601:3602]
        self.skk = self.sb("skk", [8, 128], BF16)
        self.dcol = self.sb("dcol", [8, 1], F32)

        self.xnb = [self.bigA[:, 512 * i:512 * (i + 1)] for i in range(4)]
        self.woT = self.bigA[:, 2048:10240]
        self.gbf = [self.bigA[:, 10240 + 512 * g:10240 + 512 * (g + 1)] for g in range(NG)]
        self.hs = [self.bigA[:, 1024 * f:1024 * (f + 1)] for f in range(16)]
        bAf = self.bigA.ap().bitcast(F32)
        self.x1t = [bAf[:, 512 * i:512 * (i + 1)] for i in range(4)]

        self.pb = [nc.alloc_psum_tensor(f"pb{i}", [128, 512], F32) for i in range(7)]
        self.pbt = nc.alloc_psum_tensor("pbt", [128, 1024], BF16)
        self.PBT = 7

        self.pid_sp = nc.partition_id([ET.SP])

        self.phase_init()
        tok_xn = self.phase1()
        if self.stage >= 2:
            tok_y = self.phase2(tok_xn)
        if self.stage >= 3:
            tok_z = self.phase3(tok_y)
        if self.stage >= 4:
            tok_ffn = self.phase4(tok_z)
        if self.stage >= 5:
            self.phase5(tok_ffn)
        else:
            last = tok_xn if self.stage < 2 else (tok_y if self.stage < 3 else (
                tok_z if self.stage < 4 else tok_ffn))
            lw = last if isinstance(last, list) else [last]
            for g in range(NG):
                self.dma("act", self.o_slice[128 * g:128 * (g + 1), :],
                         self.stream[g][:, 0:D], waits=lw)
        last_toks = []
        for engn in ("act", "pool", "sp"):
            seen = set()
            for waits, fn, tok in reversed(self.ops[engn]):
                if tok is not None and tok.inc == 16 and tok.sem not in seen:
                    seen.add(tok.sem)
                    tok.used = True
                    last_toks.append(tok)
                if len(seen) >= self.DMA_SEMS:
                    break
        self.emit("sp", lambda e: e.nop(), waits=last_toks)
        self.finish()
        return nc

    def phase_init(self):
        tf = self.dma("sp", self.ident_f[:], self.c_idf)
        tb = self.dma("sp", self.ident_b[:], self.c_idb)
        t8 = self.dma("sp", self.mask8[:], self.c_mask8)
        t128 = self.dma("sp", self.mask128[:], self.c_mask128)
        ton1 = self.dma("sp", self.ones1_f[:], self.c_ones[0:1, :])
        self.tok_ones1b = self.pool(lambda e: e.memset(self.ones1_b[:], 1.0))
        self.tok_onescol = self.pool(lambda e: e.memset(self.onescol_b[:], 1.0))
        self.pool(lambda e: e.memset(self.cst[:, 0:1], LN_EPS))
        self.tok_cst = self.pool(lambda e: e.memset(self.cst[:, 1:2], RMS_EPS))
        self.tok_ones1 = ton1
        self.tok_mask8 = t8
        self.tok_mask128 = t128
        self.tok_idf = tf
        self.tok_idb = tb

        toks = []
        for i, par in enumerate((self.ln_w, self.ln_b, self.b_out, self.norm_w)):
            toks.append(self.dma("sp", self.pstack[8 * i:8 * i + 8, :],
                                 par.rearrange("(g p) -> g p", p=128)))
        t1 = self.pe(lambda e: e.transpose(self.pb[0][0:128, 0:32], self.pstack[:],
                                           self.ident_f[0:32, 0:32]), waits=toks + [tf])
        self.wrote(0, t1)
        t2 = self.dve(lambda e: e.tensor_copy(self.sparams[:, 0:32], self.pb[0][0:128, 0:32]),
                      waits=[t1])
        self.read(0, t2)
        self.tok_sparams = t2

        self.tok_rw = self.dma("pool", self.rw_bf[:].rearrange("p (k e) -> p k e", k=NG),
                               self.router_w.rearrange("(k p) e -> p k e", p=128))

        tz = self.pool(lambda e: e.memset(self.sq_bf[:, 0:128], 0.0))
        self.tok_padz = self.dma("act", self.pad_k[:, 128:256], self.sq_bf[:, 0:128], waits=[tz])
        self._sqbf_war = self.tok_padz
        self._zrow_war = {0: None, 1: None}

    def pcol(self, idx, g):
        return self.sparams[:, 8 * idx + g:8 * idx + g + 1]

    def phase1(self):
        sc = self.statc
        tok_tile_ev = {}
        tok_sum = []
        tok_sumsq = []
        tok_ld_list = []
        for tt in range(NTT):
            buf = self.tmb[tt % 2]
            war = ([tok_tile_ev[(tt - 2, NG - 1)], tok_sum[tt - 2], tok_sumsq[tt - 2]]
                   if tt >= 2 else [])
            tok_ld = self.dma("sp", buf, self.xs[128 * tt:128 * (tt + 1), :], waits=war)
            tok_ld_list.append(tok_ld)
            tok_sum.append(self.dve(lambda e, buf=buf, tt=tt: e.reduce_sum(
                sc[:, tt:tt + 1], buf, axis=AX.X), waits=[tok_ld]))
            ts1 = self.dve(lambda e, buf=buf, tt=tt: e.scalar_tensor_tensor(
                self.ap32[0], buf[:, 0:512], 1.0, buf[:, 0:512], ALU.mult, ALU.mult,
                accum_out=sc[:, 16 + tt:17 + tt]),
                waits=[tok_ld, getattr(self, "_ap32war0", None)])
            ts2 = self.dve(lambda e, buf=buf, tt=tt: e.scalar_tensor_tensor(
                self.ap32[1], buf[:, 512:1024], 1.0, buf[:, 512:1024], ALU.mult, ALU.mult,
                accum_out=sc[:, 32 + tt:33 + tt]),
                waits=[tok_ld, getattr(self, "_ap32war1", None)])
            self._ap32war0 = ts1
            self._ap32war1 = ts2
            tok_sumsq.append(ts2)
            for g in range(NG):
                bank = g % 2
                tk = self.pe(lambda e, buf=buf, g=g, bank=bank: e.transpose(
                    self.pb[bank][:, 0:128], buf[:, 128 * g:128 * (g + 1)], self.ident_f[:]),
                    waits=[tok_ld, self.tok_idf] + self.bw(bank))
                self.wrote(bank, tk)
                ev = self.dve(lambda e, g=g, tt=tt, bank=bank: e.tensor_copy(
                    self.stream[g][:, 128 * tt:128 * (tt + 1)], self.pb[bank][:, 0:128]),
                    waits=[tk])
                self.read(bank, ev)
                tok_tile_ev[(tt, g)] = ev
        if self.debug:
            self.dma("sp", self.dbg_sc2, sc[:, :], waits=tok_sumsq + tok_sum)
        def dbgdump(idx, lo, tok):
            if not self.debug:
                return tok
            return self.dma("sp", self.dbg_steps[idx], sc[:, lo:lo + 16], waits=[tok])
        t1 = self.dve(lambda e: e.tensor_add(sc[:, 16:32], sc[:, 16:32], sc[:, 32:48]),
                      waits=tok_sumsq)
        d1 = dbgdump(0, 16, t1)
        t2 = self.dve(lambda e: e.tensor_scalar_mul(sc[:, 48:64], sc[:, 0:16], 1.0 / D),
                      waits=tok_sum)
        d2 = dbgdump(1, 48, t2)
        t3 = self.dve(lambda e: e.tensor_mul(sc[:, 32:48], sc[:, 48:64], sc[:, 48:64]),
                      waits=[t2])
        d3 = dbgdump(2, 32, t3)
        t4 = self.dve(lambda e: e.scalar_tensor_tensor(
            sc[:, 16:32], sc[:, 16:32], 1.0 / D, sc[:, 32:48], ALU.mult, ALU.subtract),
            waits=[t1, t3, d1, d3])
        d4 = dbgdump(3, 16, t4)
        t5 = self.act(lambda e: e.activation(sc[:, 32:48], sc[:, 16:32], AF.Sqrt,
                                             bias=self.cst[:, 0:1]),
                      waits=[t4, self.tok_cst, d3])
        d5 = dbgdump(4, 32, t5)
        t6 = self.dve(lambda e: e.reciprocal(sc[:, 64:80], sc[:, 32:48]), waits=[t5])
        d6 = dbgdump(5, 64, t6)
        t7 = self.dve(lambda e: e.scalar_tensor_tensor(
            sc[:, 80:96], sc[:, 48:64], -1.0, sc[:, 64:80], ALU.mult, ALU.mult),
            waits=[t6])
        t8 = self.pe(lambda e: e.transpose(self.pb[2][0:32, 0:128], sc[:, 64:96],
                                           self.ident_f[:]), waits=[t6, t7] + self.bw(2))
        self.wrote(2, t8)
        t9 = self.dve(lambda e: e.tensor_copy(self.srow16[:, :], self.pb[2][0:32, 0:128]),
                      waits=[t8])
        self.read(2, t9)
        tokd = self.dma("sp", self.rows_scr, self.srow16[:, :], waits=[t9])
        tokr = self.dma("sp", self.rowA[0:1, :],
                        AP(self.rows_scr.tensor, 0, [[2048, 1], [1, 2048]]), waits=[tokd])
        tokq = self.dma("sp", self.rowB[0:1, :],
                        AP(self.rows_scr.tensor, 2048, [[2048, 1], [1, 2048]]), waits=[tokd])
        self.tok_ln0 = (tokr, tokq)

        tok_w = []
        slot_tok = [None] * 4
        idx = 0
        for ti in range(NTAU):
            wb3 = self.bw(3)
            tbr = self.pe(lambda e, ti=ti: e.matmul(
                self.pb[3][:, 0:512], self.ones1_f[:], self.rowA[0:1, TAU * ti:TAU * (ti + 1)],
                start=True, stop=True), waits=[tokr, self.tok_ones1] + wb3)
            self.wrote(3, tbr)
            wb4 = self.bw(4)
            tbq = self.pe(lambda e, ti=ti: e.matmul(
                self.pb[4][:, 0:512], self.ones1_f[:], self.rowB[0:1, TAU * ti:TAU * (ti + 1)],
                start=True, stop=True), waits=[tokq] + wb4)
            self.wrote(4, tbq)
            last_r = None
            last_q = None
            for g in range(NG):
                slot = idx % 4
                scr = self.ap32[idx % 2]
                tmu = self.dve(lambda e, g=g, ti=ti, scr=scr: e.tensor_mul(
                    scr, self.stream[g][:, TAU * ti:TAU * (ti + 1)], self.pb[3][:, 0:512]),
                    waits=[tbr, tok_tile_ev[(NTT - 1, g)], slot_tok[slot],
                           getattr(self, "_ap32war" + str(idx % 2), None)])
                tadd = self.dve(lambda e, g=g, slot=slot, scr=scr: e.tensor_add(
                    self.xnb[slot], scr, self.pb[4][:, 0:512]), waits=[tmu, tbq])
                setattr(self, "_ap32war" + str(idx % 2), tadd)
                last_r, last_q = tmu, tadd
                slot_tok[slot] = self.dma(
                    "act", self.cc_xn_in[128 * g:128 * (g + 1), TAU * ti:TAU * (ti + 1)],
                    self.xnb[slot], waits=[tadd])
                tok_w.append(slot_tok[slot])
                idx += 1
            self.read(3, last_r)
            self.read(4, last_q)
        if self.debug:
            self.dma("sp", self.dbg_cst, self.cst[:, :], waits=tok_w)
            self.dma("sp", self.dbg_sc, self.statc[:, :], waits=tok_w)
            self.dma("sp", self.dbg_sr, self.srow16[:, :], waits=tok_w)
            self.dma("sp", self.dbg_xnin, self.cc_xn_in, waits=tok_w)
            self.dma("sp", self.dbg_rows[0:1, :], self.rowA[0:1, :], waits=tok_w)
            self.dma("sp", self.dbg_rows[1:2, :], self.rowB[0:1, :], waits=tok_w)
            for g in range(NG):
                self.dma("sp", self.dbg_st[g, :, :], self.stream[g][:, :], waits=tok_w)
        tok_cc = self.emit("pool", lambda e: e.collective_compute(
            "AllGather", ALU.bypass, replica_groups=[list(range(NCORES))],
            ins=[self.cc_xn_in.opt()], outs=[self.cc_xn_out.opt()]),
            waits=tok_w, sem="cc1", inc=1)
        return tok_cc

    def phase2(self, tok_xn):
        tok_wot = self.build_woT()
        self.tok_wot_last = tok_wot
        src_my = AP(self.cc_xn_out.tensor, self.pid_sp * (NCH * TCORE),
                    [[D * TCORE, NCORES], [TCORE, NCH], [1, TCORE]])
        tok_xn = self.dma("sp", self.xn_my, src_my, waits=[tok_xn])
        ytoks = []
        for g in range(NSG):
            ytoks += self.ssm_group(g, tok_xn)
        if self.debug:
            self.dma("sp", self.dbg_xn, self.xn_my, waits=ytoks)
            self.dma("sp", self.dbg_pk, self.pad_k, waits=ytoks)
            self.dma("sp", self.dbg_yin, self.cc_y_in, waits=ytoks)
        tok_ycc = self.emit("pool", lambda e: e.collective_compute(
            "AllGather", ALU.bypass, replica_groups=[list(range(NCORES))],
            ins=[self.cc_y_in.opt()], outs=[self.cc_y_out.opt()]),
            waits=ytoks, sem="cc2", inc=1)
        return tok_ycc

    def build_woT(self):
        last = None
        prev_tp = {}
        for og in range(NG):
            tok_ld = self.dma("pool", self.wo_stage,
                              self.W_out[128 * og:128 * (og + 1), :],
                              waits=[prev_tp.get(og - 1)])
            lt = None
            for ig in range(NG):
                wb = self.bw(self.PBT)
                tk = self.pe(lambda e, ig=ig: e.transpose(
                    self.pbt[:, 0:128], self.wo_stage[:, 128 * ig:128 * (ig + 1)],
                    self.ident_b[:]), waits=[tok_ld, self.tok_idb] + wb)
                self.wrote(self.PBT, tk)
                ev = self.dve(lambda e, ig=ig, og=og: e.tensor_copy(
                    self.woT[:, 1024 * ig + 128 * og:1024 * ig + 128 * (og + 1)],
                    self.pbt[:, 0:128]), waits=[tk])
                self.read(self.PBT, ev)
                lt = tk
                last = ev
            prev_tp[og] = lt
        return last

    def ssm_group(self, g, tok_xn):
        M0, T0, M1, T1 = self.mt
        tok_ac = self.dma("sp", self.acomp, AP(self.A_s.tensor, 256 * g * 8, [[16, 128], [1, NST]]))
        acomp_ap = self.acomp

        def mk_bd(e):
            a3 = self.mask128[:].rearrange("p (c m) -> p c m", c=8)
            comp = AP(acomp_ap.tensor, acomp_ap.offset,
                      [list(acomp_ap.ap[0]), [0, 8], [1, NST]])
            return e.tensor_tensor(M0.rearrange("p (c m) -> p c m", c=8), a3, comp, ALU.mult)
        tok_bd = self.dve(mk_bd, waits=[tok_ac, self.tok_mask128, getattr(self, "_mt_war", None)])
        wb2 = self.bw(2)
        tk = self.pe(lambda e: e.transpose(self.pb[2][:, 0:128], M0, self.ident_f[:]),
                     waits=[tok_bd] + wb2)
        self.wrote(2, tk)
        tok_t0 = self.dve(lambda e: e.tensor_copy(T0, self.pb[2][:, 0:128]), waits=[tk])
        self.read(2, tok_t0)

        tok_cc_ = self.dma("sp", self.ccol, AP(self.Cm_s.tensor, 128 * g, [[1, 128], [1, 1]]))
        tok_bc_ = self.dma("sp", self.bcol, AP(self.Bm_s.tensor, 128 * g, [[1, 128], [1, 1]]))
        tok_dc_ = self.dma("sp", self.dcol[:], AP(self.Dm_s.tensor, 8 * g, [[1, 8], [1, 1]]))

        xga = self.xg[g % 2]
        tok_xtiles = []
        prev_ev = getattr(self, "_xg_war", {}).get(g % 2)
        zslot_tok = [getattr(self, "_zrow_war", {}).get(i) for i in range(2)]
        for r in range(NCORES):
            b, q = r // 4, r % 4
            zs = r % 2
            src = AP(self.xn_my.tensor, (r * NCH + 8 * g) * TCORE,
                     [[TCORE, 8], [128, 16], [1, 128]])
            tok_l = self.dma("sp", self.zrow_b[zs], src, waits=[tok_xn, zslot_tok[zs]])
            wb = self.bw(self.PBT)
            tk = self.pe(lambda e, zs=zs: e.transpose(self.pbt[:, 0:128], self.zrow_b[zs],
                                                      self.ident_b[:]),
                         waits=[tok_l, self.tok_idb] + wb)
            self.wrote(self.PBT, tk)
            zslot_tok[zs] = tk
            dst = AP(xga.tensor, xga.offset + 64 * b + 16 * q,
                     [list(xga.ap[0]), [128, 8], [1, 16]])
            ev = self.dve(lambda e, dst=dst: e.tensor_copy(
                dst, self.pbt[:, 0:128].rearrange("p (c m) -> p c m", c=8)),
                waits=[tk, prev_ev])
            self.read(self.PBT, ev)
            tok_xtiles.append(ev)
        if not hasattr(self, "_zrow_war"):
            self._zrow_war = {}
        self._zrow_war[0], self._zrow_war[1] = zslot_tok[0], zslot_tok[1]

        if SSM_FULL:
            tok_pk = self.ssm_tables(g, (M0, T0, M1, T1), tok_bd, tok_t0,
                                     tok_cc_, tok_bc_, tok_dc_, tok_xtiles, xga)
        else:
            tz = self.pool(lambda e: e.memset(self.skk[:], 0.0))
            td = self.dve(lambda e: e.tensor_copy(self.skk[:, 127:128], self.dcol[:]),
                          waits=[tz, tok_dc_, getattr(self, "_skk_war", None)])
            tok_pk = self.dma("act", self.pad_k[8 * g:8 * (g + 1), 0:128], self.skk[:],
                              waits=[td, self.tok_padz])
            self._skk_war = tok_pk

        ytoks = []
        hank_war = [getattr(self, "_hank_war", {}).get(i) for i in range(2)]
        ysb_war = [getattr(self, "_ysb_war", {}).get(i) for i in range(2)]
        xm_war = getattr(self, "_xm_war", None)
        for c in range(8):
            hb = self.hank[c % 2]
            tok_h = self.dma("sp", hb,
                             AP(self.pad_k.tensor, (8 * g + c) * 256, [[1, 128], [1, 128]]),
                             waits=[tok_pk, self.tok_padz, hank_war[c % 2]])
            xd = xga[:, 128 * c:128 * (c + 1)]
            wb1 = self.bw(1)
            tok_conv = self.pe(lambda e, xd=xd, hb=hb: e.matmul(
                self.pb[1][:, 0:128], xd, hb, start=True, stop=not SSM_FULL),
                waits=[tok_h] + tok_xtiles + wb1)
            hank_war[c % 2] = tok_conv
            if SSM_FULL:
                bufi, stoks = self.scan_final
                buf = self.scanb[bufi]
                src = AP(buf.tensor, buf.offset + 31, [list(buf.ap[0]), [96, 2], [1, 64]])
                tm = self.dve(lambda e, c=c, src=src: e.tensor_scalar_mul(
                    self.xmask.rearrange("p (b k) -> p b k", b=2), src, self.mask8[:, c:c + 1]),
                    waits=list(stoks) + [self.tok_mask8, xm_war])
                tok_conv = self.pe(lambda e: e.matmul(
                    self.pb[1][:, 0:128], self.xmask, self.vrev, start=False, stop=True),
                    waits=[tm, tok_conv, self.tok_vrev])
                xm_war = tok_conv
            self.wrote(1, tok_conv)
            yb = self.ysb[c % 2]
            tok_ev = self.dve(lambda e, yb=yb: e.tensor_copy(yb[:, ::-1], self.pb[1][:, 0:128]),
                              waits=[tok_conv, ysb_war[c % 2]])
            self.read(1, tok_ev)
            ytoks.append(self.dma("act",
                                  AP(self.cc_y_in.tensor, (8 * g + c) * TTOT, [[128, 128], [1, 128]]),
                                  yb, waits=[tok_ev]))
            ysb_war[c % 2] = ytoks[-1]
        if not hasattr(self, "_hank_war"):
            self._hank_war, self._ysb_war = {}, {}
        self._hank_war[0], self._hank_war[1] = hank_war
        self._ysb_war[0], self._ysb_war[1] = ysb_war
        self._xm_war = xm_war
        self._xg_war = getattr(self, "_xg_war", {})
        self._xg_war[g % 2] = ytoks[-1]
        if SSM_FULL:
            self._scan_war0 = xm_war
            self._scan_war1 = xm_war
            self._vrev_war = xm_war
            self._mt_war = self._mt_war_next
        else:
            self._mt_war = tok_t0
        return ytoks

    def ssm_tables(self, g, mt, tok_bd, tok_t0, tok_cc_, tok_bc_, tok_dc_, tok_xtiles, xga):
        M0, T0, M1, T1 = mt
        cur_m, cur_t = M0, T0
        nxt_m, nxt_t = M1, T1
        tok_cm, tok_ct = tok_bd, tok_t0
        vf, wb_ = self.vfwd, self.wbd

        vf_war = getattr(self, "_vf_war", None)
        wb2 = self.bw(2)
        tk = self.pe(lambda e, cur_m=cur_m: e.matmul(self.pb[2][:, 0:1], cur_m, self.ccol,
                                        start=True, stop=True), waits=[tok_cm, tok_cc_] + wb2)
        self.wrote(2, tk)
        tokv = self.dve(lambda e: e.tensor_copy(vf[:, 0:1], self.pb[2][:, 0:1]),
                        waits=[tk, vf_war])
        tokvr = self.dve(lambda e: e.tensor_copy(self.vrev[:, 127:128], self.pb[2][:, 0:1]),
                         waits=[tk, getattr(self, "_vrev_war", None)])
        self.read(2, tokvr)
        tokw = self.dve(lambda e: e.tensor_copy(wb_[:, 127:128], self.bcol),
                        waits=[tok_bc_, getattr(self, "_wbd_war", None)])
        tok_pk = None
        scan_state = None
        for s in range(13):
            if s < 7:
                n = 1 << s
                wb2 = self.bw(2)
                tkv = self.pe(lambda e, n=n, cur_m=cur_m: e.matmul(
                    self.pb[2][:, 0:n], cur_m, vf[:, 0:n], start=True, stop=True),
                              waits=[tok_cm, tokv] + wb2)
                self.wrote(2, tkv)
                tokv = self.dve(lambda e, n=n: e.tensor_copy(vf[:, n:2 * n], self.pb[2][:, 0:n]),
                                waits=[tkv])
                tokvr = self.dve(lambda e, n=n: e.tensor_copy(
                    self.vrev[:, 128 - 2 * n:128 - n][:, ::-1], self.pb[2][:, 0:n]),
                    waits=[tkv])
                self.read(2, tokvr)
                wb3 = self.bw(3)
                tkw = self.pe(lambda e, n=n, cur_t=cur_t: e.matmul(
                    self.pb[3][:, 0:n], cur_t, wb_[:, 128 - n:128], start=True, stop=True),
                              waits=[tok_ct, tokw] + wb3)
                self.wrote(3, tkw)
                tokw = self.dve(lambda e, n=n: e.tensor_copy(wb_[:, 128 - 2 * n:128 - n],
                                                             self.pb[3][:, 0:n]), waits=[tkw])
                self.read(3, tokw)
            if s == 7:
                self._vf_war = None
                tok_pk = self.ssm_kk(g, tokv, tokvr, tok_bc_, tok_cc_, tok_dc_)
                scan_state = self.ssm_sloc(g, tokw, tok_xtiles, xga)
            if 7 <= s < 13:
                scan_state = self.ssm_round(s - 7, scan_state, cur_t, tok_ct)
            if s < 12:
                wb4 = self.bw(4)
                tmm = self.pe(lambda e, cur_m=cur_m, cur_t=cur_t: e.matmul(
                    self.pb[4][:, 0:128], cur_t, cur_m, start=True, stop=True),
                    waits=[tok_cm, tok_ct] + wb4)
                self.wrote(4, tmm)
                tok_nm = self.dve(lambda e, nxt_m=nxt_m: e.tensor_copy(nxt_m, self.pb[4][:, 0:128]),
                                  waits=[tmm])
                self.read(4, tok_nm)
                wb5 = self.bw(5)
                tmt = self.pe(lambda e, cur_m=cur_m, cur_t=cur_t: e.matmul(
                    self.pb[5][:, 0:128], cur_m, cur_t, start=True, stop=True),
                    waits=[tok_cm, tok_ct] + wb5)
                self.wrote(5, tmt)
                tok_nt = self.dve(lambda e, nxt_t=nxt_t: e.tensor_copy(nxt_t, self.pb[5][:, 0:128]),
                                  waits=[tmt])
                self.read(5, tok_nt)
                cur_m, nxt_m = nxt_m, cur_m
                cur_t, nxt_t = nxt_t, cur_t
                tok_cm, tok_ct = tok_nm, tok_nt
        self.scan_final = scan_state
        self._mt_war_next = scan_state[1][-1]
        self.tok_vrev = tokvr
        return tok_pk

    def ssm_kk(self, g, tokv, tokvr, tok_bc_, tok_cc_, tok_dc_):
        tok_bsel = self.dve(lambda e: e.tensor_scalar_mul(self.cwf[:, 0:8], self.mask8[:],
                                                          self.bcol[:, 0:1]),
                            waits=[tok_bc_, self.tok_mask8, getattr(self, "_bsel_war", None)])
        wb6 = self.bw(6)
        t1 = self.pe(lambda e: e.matmul(self.pb[6][0:8, 0:1], self.cwf[:, 0:8], self.ccol,
                                        start=True, stop=True), waits=[tok_bsel, tok_cc_] + wb6)
        t2 = self.pe(lambda e: e.matmul(self.pb[6][0:8, 1:128], self.cwf[:, 0:8], self.vfwd[:, 0:127],
                                        start=True, stop=True), waits=[tok_bsel, tokv])
        self.wrote(6, t2)
        self._vf_war = t2
        self._bsel_war = t2
        t3 = self.dve(lambda e: e.tensor_copy(self.skk[:, ::-1], self.pb[6][0:8, 0:128]),
                      waits=[t1, t2, getattr(self, "_skk_war", None)])
        self.read(6, t3)
        t4 = self.dve(lambda e: e.tensor_add(self.skk[:, 127:128], self.skk[:, 127:128],
                                             self.dcol[:]), waits=[t3, tok_dc_])
        tok_pk = self.dma("act", self.pad_k[8 * g:8 * (g + 1), 0:128], self.skk[:],
                          waits=[t4])
        self._skk_war = tok_pk
        return tok_pk

    def ssm_sloc(self, g, tokw, tok_xtiles, xga):
        wb2 = self.bw(2)
        tk = self.pe(lambda e: e.transpose(self.pb[2][:, 0:128], self.wbd, self.ident_f[:]),
                     waits=[tokw] + wb2)
        self.wrote(2, tk)
        tok_wf = self.dve(lambda e: e.tensor_copy(self.wflat, self.pb[2][:, 0:128]),
                          waits=[tk, getattr(self, "_wflat_war", None)])
        self.read(2, tok_wf)
        self._wbd_war = tok_wf
        wb3 = self.bw(3)
        stoks = []
        for c in range(8):
            xd = xga[:, 128 * c:128 * (c + 1)]
            stoks.append(self.pe(lambda e, c=c, xd=xd: e.matmul(
                self.pb[3][:, 16 * c:16 * (c + 1)], xd, self.wflat[:, 16 * c:16 * (c + 1)],
                start=True, stop=True), waits=[tok_wf] + tok_xtiles + (wb3 if c == 0 else [])))
        self.wrote(3, stoks[-1])
        self._wflat_war = stoks[-1]
        te = self.dve(lambda e: e.tensor_copy(self.sgev, self.pb[3][:, 0:128]), waits=stoks + [getattr(self, "_sgev_war", None)])
        self.read(3, te)
        wb2 = self.bw(2)
        tk2 = self.pe(lambda e: e.transpose(self.pb[2][:, 0:128], self.sgev, self.ident_f[:]),
                      waits=[te] + wb2)
        self.wrote(2, tk2)
        self._sgev_war = tk2
        tz0 = self.pool(lambda e: e.memset(self.scanb[0][0:128, 0:192], 0.0),
                        waits=[getattr(self, "_scan_war0", None)])
        tz1 = self.pool(lambda e: e.memset(self.scanb[1][0:128, 0:192], 0.0),
                        waits=[getattr(self, "_scan_war1", None)])
        dst = AP(self.scanb[0].tensor, self.scanb[0].offset + 32,
                 [list(self.scanb[0].ap[0]), [96, 2], [1, 64]])
        tev = self.dve(lambda e, dst=dst: e.tensor_copy(
            dst, self.pb[2][:, 0:128].rearrange("p (b k) -> p b k", b=2)),
            waits=[tk2, tz0])
        self.read(2, tev)
        return (0, (tev, tz1))

    def ssm_round(self, t, scan_state, cur_t, tok_ct):
        bi, toks = scan_state
        src, dst = self.scanb[bi], self.scanb[1 - bi]
        sh = 1 << t
        rtoks = []
        for b in range(2):
            wb4 = self.bw(4)
            tk = self.pe(lambda e, b=b, sh=sh, src=src, cur_t=cur_t: e.matmul(
                self.pb[4][:, 64 * b:64 * (b + 1)], cur_t,
                src[:, 32 + 96 * b - sh:32 + 96 * b - sh + 64],
                start=True, stop=True),
                waits=[tok_ct] + list(toks) + (wb4 if b == 0 else []))
            self.wrote(4, tk)
            rtoks.append(self.dve(lambda e, b=b, src=src, dst=dst: e.tensor_add(
                dst[:, 32 + 96 * b:32 + 96 * b + 64],
                self.pb[4][:, 64 * b:64 * (b + 1)],
                src[:, 32 + 96 * b:32 + 96 * b + 64]), waits=[tk]))
        self.read(4, rtoks[-1])
        return (1 - bi, tuple(rtoks))

    def phase3(self, tok_y):
        rowC, rowD = self.rowC, self.rowD
        tok_y = self.dma("sp", self.y_my,
                         AP(self.cc_y_out.tensor, self.pid_sp * TCORE,
                            [[TTOT, D], [1, TCORE]]), waits=[tok_y])
        x1_toks = []
        for ti in range(NTAU):
            tok_y_ld = []
            for g in range(NG):
                src = AP(self.y_my.tensor, (128 * g) * TCORE + TAU * ti,
                         [[TCORE, 128], [1, TAU]])
                tok_y_ld.append(self.dma("sp", self.ytile[g], src,
                                         waits=[tok_y, getattr(self, f"_yt_war{g}", None)]))
            if self.debug:
                for g in range(NG):
                    self.dma("act", self.dbg_y[g, :, TAU * ti:TAU * (ti + 1)],
                             self.ytile[g], waits=[tok_y_ld[g]])
            t_st = self.fm_stats([self.ytile[g] for g in range(NG)], tok_y_ld, ti,
                                 rowC, rowD, eps=LN_EPS, full=False)
            tbr = self.bcast(3, rowC, ti, [t_st])
            tbq = self.bcast(4, rowD, ti, [t_st])
            g_toks = []
            for g in range(NG):
                scr = self.ap32[g % 2]
                t1 = self.dve(lambda e, g=g, scr=scr: e.tensor_mul(
                    scr, self.ytile[g], self.pb[3][:, 0:512]),
                    waits=[tbr, tok_y_ld[g], getattr(self, f"_ap32war{g % 2}", None)])
                t2 = self.dve(lambda e, g=g, scr=scr: e.scalar_tensor_tensor(
                    scr, scr, 1.0, self.pb[4][:, 0:512], ALU.mult, ALU.add),
                    waits=[t1, tbq])
                t3 = self.dve(lambda e, g=g, scr=scr: e.tensor_scalar(
                    scr, scr, self.pcol(0, g), self.pcol(1, g), ALU.mult, ALU.add),
                    waits=[t2, self.tok_sparams])
                tg = self.act(lambda e, g=g, scr=scr: e.activation(
                    self.gbf[g], scr, AF.Gelu), waits=[t3])
                setattr(self, f"_ap32war{g % 2}", tg)
                g_toks.append(tg)
            self.read(3, g_toks[-1])
            self.read(4, g_toks[-1])
            att_toks = []
            tbr0 = self.bcast(3, self.rowA, ti, [self.tok_ln0[0], g_toks[-1]])
            tbq0 = self.bcast(4, self.rowB, ti, [self.tok_ln0[1]])
            for og in range(NG):
                bank = og % 2
                wbk = self.bw(bank)
                mms = []
                for ig in range(NG):
                    mms.append(self.pe(lambda e, og=og, ig=ig, bank=bank: e.matmul(
                        self.pb[bank][:, 0:512],
                        self.woT[:, 1024 * ig + 128 * og:1024 * ig + 128 * (og + 1)],
                        self.gbf[ig], start=(ig == 0), stop=(ig == NG - 1)),
                        waits=[g_toks[ig], self.tok_wot_last] + (wbk if ig == 0 else [])))
                self.wrote(bank, mms[-1])
                t1 = self.dve(lambda e, og=og, bank=bank: e.tensor_scalar(
                    self.att[og], self.pb[bank][:, 0:512], self.pcol(2, og), None, ALU.add),
                    waits=[mms[-1]])
                self.read(bank, t1)
                t2 = self.dve(lambda e, og=og, ti=ti: e.tensor_mul(
                    self.ytile[og], self.stream[og][:, TAU * ti:TAU * (ti + 1)],
                    self.pb[3][:, 0:512]), waits=[tbr0])
                t3 = self.dve(lambda e, og=og: e.scalar_tensor_tensor(
                    self.ytile[og], self.ytile[og], 1.0, self.pb[4][:, 0:512],
                    ALU.mult, ALU.add), waits=[t2, tbq0])
                att_toks.append(self.dve(lambda e, og=og: e.tensor_add(
                    self.att[og], self.att[og], self.ytile[og]), waits=[t1, t3]))
                setattr(self, f"_yt_war{og}", att_toks[-1])
            self.read(3, att_toks[-1])
            self.read(4, att_toks[-1])
            t_st2 = self.fm_stats([self.att[g] for g in range(NG)], att_toks, ti,
                                  rowC, rowD, eps=LN_EPS, full=False)
            tbr2 = self.bcast(3, rowC, ti, [t_st2])
            tbq2 = self.bcast(4, rowD, ti, [t_st2])
            l2 = []
            for g in range(NG):
                scr = self.ap32[g % 2]
                t1 = self.dve(lambda e, g=g, scr=scr: e.tensor_mul(
                    scr, self.att[g], self.pb[3][:, 0:512]),
                    waits=[tbr2, att_toks[g], getattr(self, f"_ap32war{g % 2}", None)])
                t2 = self.dve(lambda e, g=g, scr=scr: e.scalar_tensor_tensor(
                    scr, scr, 1.0, self.pb[4][:, 0:512], ALU.mult, ALU.add),
                    waits=[t1, tbq2])
                t3 = self.dve(lambda e, g=g, ti=ti, scr=scr: e.tensor_add(
                    self.stream[g][:, TAU * ti:TAU * (ti + 1)],
                    self.stream[g][:, TAU * ti:TAU * (ti + 1)], scr), waits=[t2])
                setattr(self, f"_ap32war{g % 2}", t3)
                x1_toks.append(self.dma("act", self.x1_dram[g, :, TAU * ti:TAU * (ti + 1)],
                                        self.stream[g][:, TAU * ti:TAU * (ti + 1)], waits=[t3]))
                l2.append(t3)
            self.read(3, l2[-1])
            self.read(4, l2[-1])
            if self.debug:
                for g in range(NG):
                    self.dma("act", self.dbg_z1[g, :, TAU * ti:TAU * (ti + 1)],
                             self.stream[g][:, TAU * ti:TAU * (ti + 1)], waits=[l2[g]])
        z_toks = []
        for ti in range(NTAU):
            t_st3 = self.fm_stats([self.stream[g] for g in range(NG)], x1_toks, ti,
                                  rowC, rowD, eps=LN_EPS, full=True)
            tbr3 = self.bcast(3, rowC, ti, [t_st3])
            tbq3 = self.bcast(4, rowD, ti, [t_st3])
            l3 = []
            for g in range(NG):
                scr = self.ap32[g % 2]
                t1 = self.dve(lambda e, g=g, ti=ti, scr=scr: e.tensor_mul(
                    scr, self.stream[g][:, TAU * ti:TAU * (ti + 1)], self.pb[3][:, 0:512]),
                    waits=[tbr3, getattr(self, f"_ap32war{g % 2}", None)])
                t2 = self.dve(lambda e, g=g, ti=ti, scr=scr: e.scalar_tensor_tensor(
                    self.stream[g][:, TAU * ti:TAU * (ti + 1)], scr, 1.0,
                    self.pb[4][:, 0:512], ALU.mult, ALU.add), waits=[t1, tbq3])
                setattr(self, f"_ap32war{g % 2}", t2)
                l3.append(t2)
                z_toks.append(t2)
            self.read(3, l3[-1])
            self.read(4, l3[-1])
        return z_toks

    def bcast(self, bank, row, ti, waits):
        wb = self.bw(bank)
        tk = self.pe(lambda e: e.matmul(self.pb[bank][:, 0:512], self.ones1_f[:],
                                        row[0:1, TAU * ti:TAU * (ti + 1)], start=True, stop=True),
                     waits=list(waits) + wb + [self.tok_ones1])
        self.wrote(bank, tk)
        return tk

    def fm_stats(self, tiles, waits, ti, row_r, row_q, eps, full, rms=False):
        toks_sum = []
        toks_sq = []
        wb6 = self.bw(6)
        wb2 = self.bw(2)
        prev_c = getattr(self, "_sqbf_war", None)
        prev_s = getattr(self, "_sqbf2_war", None)
        for g in range(NG):
            tile = tiles[g]
            src = tile[:, TAU * ti:TAU * (ti + 1)] if full else tile
            tc = self.dve(lambda e, src=src: e.tensor_copy(self.sq_bf[:], src),
                          waits=list(waits) + [prev_c])
            tsq = self.act(lambda e, src=src: e.activation(self.sq_bf2[:], src, AF.Square),
                           waits=list(waits) + [prev_s])
            m1 = self.pe(lambda e, g=g: e.matmul(self.pb[6][0:1, 0:512], self.onescol_b[:],
                                                 self.sq_bf[:], start=(g == 0), stop=(g == NG - 1),
                                                 skip_group_check=True),
                         waits=[tc, self.tok_onescol] + (wb6 if g == 0 else []))
            m2 = self.pe(lambda e, g=g: e.matmul(self.pb[2][0:1, 0:512], self.onescol_b[:],
                                                 self.sq_bf2[:], start=(g == 0), stop=(g == NG - 1),
                                                 skip_group_check=True),
                         waits=[tsq] + (wb2 if g == 0 else []))
            prev_c, prev_s = m1, m2
            toks_sum.append(m1)
            toks_sq.append(m2)
        self._sqbf_war = prev_c
        self._sqbf2_war = prev_s
        self.wrote(6, toks_sum[-1])
        self.wrote(2, toks_sq[-1])
        rr = row_r[0:1, TAU * ti:TAU * (ti + 1)]
        rq = row_q[0:1, TAU * ti:TAU * (ti + 1)]
        if rms:
            t3 = self.dve(lambda e: e.tensor_scalar_mul(rr, self.pb[2][0:1, 0:512], 1.0 / D),
                          waits=[toks_sq[-1]])
            self.read(2, t3)
            self.read(6, t3)
        else:
            t1 = self.dve(lambda e: e.tensor_scalar_mul(rq, self.pb[6][0:1, 0:512], 1.0 / D),
                          waits=[toks_sum[-1]])
            self.read(6, t1)
            musq = self.ap32[1][0:1, 0:512]
            t2 = self.dve(lambda e, musq=musq: e.tensor_mul(musq, rq, rq),
                          waits=[t1, getattr(self, "_ap32war1", None)])
            t3 = self.dve(lambda e, musq=musq: e.scalar_tensor_tensor(
                rr, self.pb[2][0:1, 0:512], 1.0 / D, musq,
                ALU.mult, ALU.subtract), waits=[t2, toks_sq[-1]])
            self._ap32war1 = t3
            self.read(2, t3)
        epsc = self.cst[0:1, 0:1] if eps == LN_EPS else self.cst[0:1, 1:2]
        musq2 = self.ap32[1][0:1, 0:512]
        t4 = self.act(lambda e, epsc=epsc, musq2=musq2: e.activation(
            musq2, rr, AF.Sqrt, bias=epsc),
            waits=[t3, self.tok_cst, getattr(self, "_ap32war1", None)])
        t5 = self.dve(lambda e, musq2=musq2: e.reciprocal(rr, musq2), waits=[t4])
        self._ap32war1 = t5
        if rms:
            return t5
        t6 = self.dve(lambda e: e.scalar_tensor_tensor(rq, rq, -1.0, rr,
                                                       ALU.mult, ALU.mult), waits=[t5])
        return t6

    def phase4(self, tok_z):
        znbf = [self.warena.ap()[:, 2048 * g:2048 * (g + 1)] for g in range(NG)]
        w13 = self.warena.ap()[:, 16384:20480]
        last = tok_z
        for loop in range(2):
            last = self.ffn_loop(loop, znbf, w13, last)
        return last

    def ffn_loop(self, loop, znbf, w13, tok_prev):
        rowC, rowD = self.rowC, self.rowD
        tok_znbf = {}
        for ti in range(NTAU):
            t_st = self.fm_stats([self.stream[g] for g in range(NG)], list(tok_prev), ti,
                                 rowC, rowD, eps=RMS_EPS, full=True, rms=True)
            tbr = self.bcast(3, rowC, ti, [t_st])
            zt = []
            for g in range(NG):
                scr = self.ap32[g % 2]
                t1 = self.dve(lambda e, g=g, ti=ti, scr=scr: e.tensor_mul(
                    scr, self.stream[g][:, TAU * ti:TAU * (ti + 1)], self.pb[3][:, 0:512]),
                    waits=[tbr, getattr(self, f"_ap32war{g % 2}", None),
                           getattr(self, f"_znbf_war{g}_{ti}", None)])
                t2 = self.dve(lambda e, g=g, ti=ti, scr=scr: e.tensor_scalar_mul(
                    znbf[g][:, TAU * ti:TAU * (ti + 1)], scr, self.pcol(3, g)),
                    waits=[t1, self.tok_sparams])
                setattr(self, f"_ap32war{g % 2}", t2)
                tok_znbf[(g, ti)] = t2
                zt.append(t2)
            self.read(3, zt[-1])
        tok_znbf_all = list(tok_znbf.values())
        if loop == 0 and isinstance(tok_prev, list) and tok_prev:
            self._w13war0 = self._w13war1 = tok_prev[-1]
        tok_cw = self.ffn_routing(loop, znbf, tok_znbf)
        zup = []
        for e_i in range(NEXP):
            zup = self.ffn_expert(loop, e_i, znbf, w13, tok_znbf_all, tok_cw, zup)
        return zup

    def ffn_routing(self, loop, znbf, tok_znbf):
        sc = self.statc
        toks = []
        for tt in range(NTT):
            ti = tt // 4
            wb6 = self.bw(6)
            mms = []
            for kg in range(NG):
                mms.append(self.pe(lambda e, tt=tt, kg=kg: e.matmul(
                    self.pb[6][0:128, 0:8],
                    znbf[kg][:, 128 * tt:128 * (tt + 1)],
                    self.rw_bf[:, 8 * kg:8 * (kg + 1)],
                    start=(kg == 0), stop=(kg == NG - 1)),
                    waits=[tok_znbf[(kg, ti)], self.tok_rw] + (wb6 if kg == 0 else [])))
            self.wrote(6, mms[-1])
            cwt = self.cwf[:, 8:16]
            t1 = self.dve(lambda e: e.reduce_max(sc[:, 0:1], self.pb[6][0:128, 0:8], axis=AX.X),
                          waits=[mms[-1], getattr(self, "_cwf_war", None)])
            t1b = self.dve(lambda e: e.tensor_scalar_mul(sc[:, 0:1], sc[:, 0:1], -1.0), waits=[t1])
            t2 = self.act(lambda e, cwt=cwt: e.activation(cwt, self.pb[6][0:128, 0:8], AF.Exp,
                                                          bias=sc[:, 0:1]), waits=[t1b])
            self.read(6, t2)
            t3 = self.dve(lambda e, cwt=cwt: e.reduce_sum(sc[:, 1:2], cwt, axis=AX.X), waits=[t2])
            t4 = self.dve(lambda e: e.reciprocal(sc[:, 4:5], sc[:, 1:2]), waits=[t3])
            t5 = self.dve(lambda e, cwt=cwt: e.tensor_scalar_mul(cwt, cwt, sc[:, 4:5]), waits=[t4])
            t6 = self.dve(lambda e, cwt=cwt: e.reduce_max(sc[:, 0:1], cwt, axis=AX.X), waits=[t5])
            t7 = self.dve(lambda e, cwt=cwt: e.tensor_scalar(sc[:, 8:16], cwt, sc[:, 0:1], None,
                                                             ALU.is_ge), waits=[t6])
            t8 = self.dve(lambda e, cwt=cwt: e.scalar_tensor_tensor(
                sc[:, 24:32], sc[:, 8:16], -1e30, cwt, ALU.mult, ALU.add), waits=[t7])
            t9 = self.dve(lambda e: e.reduce_max(sc[:, 2:3], sc[:, 24:32], axis=AX.X), waits=[t8])
            t10 = self.dve(lambda e: e.tensor_add(sc[:, 3:4], sc[:, 0:1], sc[:, 2:3]), waits=[t9])
            t11 = self.dve(lambda e: e.tensor_scalar_add(sc[:, 3:4], sc[:, 3:4], TOPK_EPS),
                           waits=[t10])
            t12 = self.dve(lambda e: e.reciprocal(sc[:, 5:6], sc[:, 3:4]), waits=[t11])
            t13 = self.dve(lambda e, cwt=cwt: e.tensor_scalar(sc[:, 8:16], cwt, sc[:, 2:3], None,
                                                              ALU.is_ge), waits=[t9, t12])
            t14 = self.dve(lambda e, cwt=cwt: e.tensor_mul(cwt, cwt, sc[:, 8:16]), waits=[t13])
            t15 = self.dve(lambda e, cwt=cwt: e.tensor_scalar_mul(cwt, cwt, sc[:, 5:6]),
                           waits=[t14])
            wb5 = self.bw(5)
            t16 = self.pe(lambda e, cwt=cwt: e.transpose(self.pb[5][0:8, 0:128], cwt,
                                                         self.ident_f[:]), waits=[t15] + wb5)
            self.wrote(5, t16)
            self._cwf_war = t16
            tev = self.dve(lambda e, tt=tt: e.tensor_copy(self.cw8[:, 128 * tt:128 * (tt + 1)],
                                                          self.pb[5][0:8, 0:128]), waits=[t16])
            self.read(5, tev)
            toks.append(tev)
        return toks

    def ffn_expert(self, loop, e_i, znbf, w13, tok_znbf_all, tok_cw, prev_zup):
        tok_cr = self.dma("act", self.cwrow[0:1, :], self.cw8[e_i:e_i + 1, :],
                          waits=tok_cw + [getattr(self, "_cwrow_war", None)])
        zup = list(prev_zup)
        for th in range(2):
            tis = (2 * th, 2 * th + 1)
            cb_toks = []
            for i, ti in enumerate(tis):
                wb6 = self.bw(6)
                tk = self.pe(lambda e, ti=ti: e.matmul(
                    self.pb[6][:, 0:512], self.ones1_b[:],
                    self.cwrow[0:1, TAU * ti:TAU * (ti + 1)], start=True, stop=True),
                    waits=[tok_cr, self.tok_ones1b] + wb6 + [getattr(self, "_cwbc_war", None)])
                self.wrote(6, tk)
                ev = self.dve(lambda e, i=i: e.tensor_copy(self.cwbc[:, 512 * i:512 * (i + 1)],
                                                           self.pb[6][:, 0:512]), waits=[tk])
                self.read(6, ev)
                cb_toks.append(ev)
            self._cwrow_war = cb_toks[-1]

            hs_toks = []
            for fc in range(16):
                dbl = fc % 2
                w1p = w13[:, 1024 * dbl:1024 * (dbl + 1)]
                w3p = w13[:, 2048 + 1024 * dbl:2048 + 1024 * (dbl + 1)]
                war = [getattr(self, f"_w13war{dbl}", None)]
                src1 = AP(self.w1.tensor, e_i * D * FF + 128 * fc,
                          [[FF, 128], [128 * FF, 8], [1, 128]])
                src3 = AP(self.w3.tensor, e_i * D * FF + 128 * fc,
                          [[FF, 128], [128 * FF, 8], [1, 128]])
                tw1 = self.dma("pool", w1p.rearrange("p (k f) -> p k f", k=8), src1, waits=war)
                tw3 = self.dma("pool", w3p.rearrange("p (k f) -> p k f", k=8), src3, waits=war)
                last_mm = None
                for i, ti in enumerate(tis):
                    wb0 = self.bw(0)
                    m1 = []
                    for k in range(8):
                        m1.append(self.pe(lambda e, k=k, ti=ti, w1p=w1p: e.matmul(
                            self.pb[0][:, 0:512], w1p[:, 128 * k:128 * (k + 1)],
                            znbf[k][:, TAU * ti:TAU * (ti + 1)],
                            start=(k == 0), stop=(k == 7)),
                            waits=[tw1] + tok_znbf_all + (wb0 if k == 0 else [])))
                    self.wrote(0, m1[-1])
                    wb1 = self.bw(1)
                    m3 = []
                    for k in range(8):
                        m3.append(self.pe(lambda e, k=k, ti=ti, w3p=w3p: e.matmul(
                            self.pb[1][:, 0:512], w3p[:, 128 * k:128 * (k + 1)],
                            znbf[k][:, TAU * ti:TAU * (ti + 1)],
                            start=(k == 0), stop=(k == 7)),
                            waits=[tw3] + tok_znbf_all + (wb1 if k == 0 else [])))
                    self.wrote(1, m3[-1])
                    ts = self.act(lambda e: e.activation(self.silu_t[:], self.pb[0][:, 0:512],
                                                         AF.Silu),
                                  waits=[m1[-1], getattr(self, "_silu_war", None)])
                    self.read(0, ts)
                    tg = self.dve(lambda e: e.tensor_mul(self.glu_t[:], self.silu_t[:],
                                                         self.pb[1][:, 0:512]),
                                  waits=[ts, m3[-1], getattr(self, "_glu_war", None)])
                    self.read(1, tg)
                    self._silu_war = tg
                    th_tok = self.pool(lambda e, fc=fc, i=i: e.tensor_mul(
                        self.hs[fc][:, 512 * i:512 * (i + 1)], self.glu_t[:],
                        self.cwbc[:, 512 * i:512 * (i + 1)]),
                        waits=[tg, cb_toks[i], getattr(self, f"_hs_war{fc}_{i}", None)])
                    self._glu_war = th_tok
                    hs_toks.append(th_tok)
                    last_mm = m3[-1]
                setattr(self, f"_w13war{dbl}", last_mm)
            for jh in range(2):
                tw2 = None
                for kc in range(16):
                    src = AP(self.w2.tensor, e_i * FF * D + 128 * kc * D + 512 * jh,
                             [[D, 128], [128, 4], [1, 128]])
                    tw2 = self.dma("pool",
                                   self.w2buf[:, 512 * kc:512 * (kc + 1)].rearrange(
                                       "p (j f) -> p j f", j=4),
                                   src, waits=[getattr(self, "_w2war", None)] if kc == 0 else ())
                last = None
                for i, ti in enumerate(tis):
                    for j in range(4):
                        jj = 4 * jh + j
                        bank = 2 + ((i * 4 + j) % 2)
                        wbk = self.bw(bank)
                        mms = []
                        for k in range(16):
                            mms.append(self.pe(lambda e, k=k, j=j, i=i, bank=bank: e.matmul(
                                self.pb[bank][:, 0:512],
                                self.w2buf[:, 512 * k + 128 * j:512 * k + 128 * (j + 1)],
                                self.hs[k][:, 512 * i:512 * (i + 1)],
                                start=(k == 0), stop=(k == 15)),
                                waits=[tw2] + hs_toks + (wbk if k == 0 else [])))
                        self.wrote(bank, mms[-1])
                        tz = self.dve(lambda e, jj=jj, ti=ti, bank=bank: e.tensor_add(
                            self.stream[jj][:, TAU * ti:TAU * (ti + 1)],
                            self.stream[jj][:, TAU * ti:TAU * (ti + 1)],
                            self.pb[bank][:, 0:512]), waits=[mms[-1]])
                        self.read(bank, tz)
                        zup.append(tz)
                        last = tz
                self._w2war = last
            for fc in range(16):
                for i in range(2):
                    setattr(self, f"_hs_war{fc}_{i}", self._w2war)
            self._cwbc_war = self._w2war
        return zup

    def phase5(self, tok_ffn):
        rowC, rowD = self.rowC, self.rowD
        out_toks = []
        x1_slot = [None] * 4
        tm_war = [None, None]
        for ti in range(NTAU):
            t_st = self.fm_stats([self.stream[g] for g in range(NG)], list(tok_ffn), ti,
                                 rowC, rowD, eps=LN_EPS, full=True)
            tbr = self.bcast(3, rowC, ti, [t_st])
            tbq = self.bcast(4, rowD, ti, [t_st])
            fin = []
            for g in range(NG):
                slot = g % 4
                tok_x1 = self.dma("sp", self.x1t[slot], self.x1_dram[g, :, TAU * ti:TAU * (ti + 1)],
                                  waits=list(tok_ffn) + [x1_slot[slot]])
                scr = self.ap32[g % 2]
                t1 = self.dve(lambda e, g=g, ti=ti, scr=scr: e.tensor_mul(
                    scr, self.stream[g][:, TAU * ti:TAU * (ti + 1)], self.pb[3][:, 0:512]),
                    waits=[tbr, getattr(self, f"_ap32war{g % 2}", None)])
                t2 = self.dve(lambda e, scr=scr: e.scalar_tensor_tensor(
                    scr, scr, 1.0, self.pb[4][:, 0:512], ALU.mult, ALU.add),
                    waits=[t1, tbq])
                t3 = self.dve(lambda e, g=g, ti=ti, scr=scr, slot=slot: e.tensor_add(
                    self.stream[g][:, TAU * ti:TAU * (ti + 1)], self.x1t[slot], scr),
                    waits=[t2, tok_x1])
                setattr(self, f"_ap32war{g % 2}", t3)
                x1_slot[slot] = t3
                fin.append(t3)
            self.read(3, fin[-1])
            self.read(4, fin[-1])
            for st in range(4):
                tt = 4 * ti + st
                buf = self.tmb[tt % 2]
                evs = []
                for g in range(NG):
                    bank = g % 2
                    wbk = self.bw(bank)
                    tk = self.pe(lambda e, g=g, ti=ti, st=st, bank=bank: e.transpose(
                        self.pb[bank][:, 0:128],
                        self.stream[g][:, TAU * ti + 128 * st:TAU * ti + 128 * (st + 1)],
                        self.ident_f[:]), waits=[fin[g]] + wbk)
                    self.wrote(bank, tk)
                    ev = self.dve(lambda e, g=g, buf=buf, bank=bank: e.tensor_copy(
                        buf[:, 128 * g:128 * (g + 1)], self.pb[bank][:, 0:128]),
                        waits=[tk, tm_war[tt % 2]])
                    self.read(bank, ev)
                    evs.append(ev)
                tok_o = self.dma("act", self.o_slice[128 * tt:128 * (tt + 1), :], buf, waits=evs)
                tm_war[tt % 2] = tok_o
                out_toks.append(tok_o)
        return out_toks


def kernel(**inputs):
    b = Bld()
    nc = b.build()
    x = np.ascontiguousarray(inputs["L0_attn_x"].reshape(TTOT, D))
    import ml_dtypes
    c_idf = np.eye(128, dtype=np.float32)
    c_idb = np.eye(128).astype(ml_dtypes.bfloat16)
    p_ = np.arange(128)
    c_mask8 = (p_[:, None] // 16 == np.arange(8)[None, :]).astype(np.float32)
    c_mask128 = np.zeros((128, 128), np.float32)
    for c_ in range(8):
        c_mask128[16 * c_:16 * (c_ + 1), 16 * c_:16 * (c_ + 1)] = 1.0
    c_ones = np.ones((2, 128), np.float32)
    in_maps = []
    for c in range(NCORES):
        in_maps.append(dict(
            c_idf=c_idf, c_idb=c_idb, c_mask8=c_mask8, c_mask128=c_mask128, c_ones=c_ones,
            xs=np.ascontiguousarray(x[TCORE * c:TCORE * (c + 1)]),
            A_s=np.ascontiguousarray(inputs["L0_attn_A"][NCH * c:NCH * (c + 1)]),
            Bm_s=np.ascontiguousarray(inputs["L0_attn_Bm"][NCH * c:NCH * (c + 1)]),
            Cm_s=np.ascontiguousarray(inputs["L0_attn_Cm"][NCH * c:NCH * (c + 1)]),
            Dm_s=np.ascontiguousarray(inputs["L0_attn_Dm"][NCH * c:NCH * (c + 1)]),
            ln_w=np.ascontiguousarray(inputs["L0_attn_ln_w"]),
            ln_b=np.ascontiguousarray(inputs["L0_attn_ln_b"]),
            W_out=np.ascontiguousarray(inputs["L0_attn_W_out"]),
            b_out=np.ascontiguousarray(inputs["L0_attn_b_out"]),
            norm_w=np.ascontiguousarray(inputs["L0_ffn_norm_w"]),
            router_w=np.ascontiguousarray(inputs["L0_ffn_router_w"]),
            w1=np.ascontiguousarray(inputs["L0_ffn_w1"]),
            w3=np.ascontiguousarray(inputs["L0_ffn_w3"]),
            w2=np.ascontiguousarray(inputs["L0_ffn_w2"]),
        ))
    res = run_bass_kernel_spmd(nc, in_maps, core_ids=list(range(NCORES)))
    out = np.concatenate([res.results[c]["o_slice"] for c in range(NCORES)], axis=0)
    return out.reshape(BATCH, SQ, D)
